# version 05
# baseline (speedup 1.0000x reference)
import contextlib
import os

import numpy as np
import ml_dtypes

import concourse.bass as bass
import concourse.mybir as mybir
from concourse.bass_utils import run_bass_kernel_spmd

F32 = mybir.dt.float32
BF16 = mybir.dt.bfloat16
I8 = mybir.dt.int8
AF = mybir.ActivationFunctionType
ALU = mybir.AluOpType
AX = mybir.AxisListType

S = 4096
D = 1024
DL = 256
DH = 4096
QW = 2176
QMAIN = 2048
NQT = 17
ADEC = float(np.float32(0.01) * np.float32(0.01))
SCALE = 1.0 / 32.0
EBIAS = -4.0
EPS = 1e-5
R127 = float(np.float32(1.0 / 127.0))
TF = 168
HW = TF + 2
FT = []
_f0 = 0
while _f0 < QW:
    FT.append((_f0, min(TF, QW - _f0)))
    _f0 += TF


class Sem:
    def __init__(self, st, nc, name):
        self.h = st.enter_context(nc.semaphore(name))
        self.n = 0
        self.name = name

    @property
    def now(self):
        return (self, self.n)


def _w(evts):
    return [ev for ev in evts if ev is not None]


class KB:
    def __init__(self, nc, sems):
        self.nc = nc
        self.sems = sems
        self.prog = {"sync": [], "tensor": [], "vector": [], "scalar": [], "gpsimd": []}

    def op(self, eng, fn, waits=(), inc=None, inck=1):
        waits = [(s.h, v) for (s, v) in waits if s is not None and v > 0]
        inc_h = None
        if inc is not None:
            inc_h = inc.h
            inc.n += inck

        def run(e, fn=fn, waits=waits, inc_h=inc_h, inck=inck):
            for h, v in waits:
                e.wait_ge(h, v)
            inst = fn(e)
            if inc_h is not None:
                inst.then_inc(inc_h, inck)
        self.prog[eng].append(run)

    def wait(self, eng, *waits):
        self.op(eng, lambda e: None, waits=waits)

    def barrier_all(self):
        evts = [s.now for s in self.sems]
        for eng in self.prog:
            self.wait(eng, *evts)

    def emit(self, block):
        for eng in ("sync", "tensor", "vector", "scalar", "gpsimd"):
            ops = self.prog[eng]
            getattr(block, eng)(lambda e, ops=ops: [f(e) for f in ops])


def build_nc():
    PH = int(os.environ.get("KPH", "9"))
    KNODIAG = int(os.environ.get("KNODIAG", "0"))
    KNOEXP = int(os.environ.get("KNOEXP", "0"))
    KNOPV = int(os.environ.get("KNOPV", "0"))
    KNOQV = int(os.environ.get("KNOQV", "0"))
    nc = bass.Bass("TRN2", target_bir_lowering=False, debug=False)
    st = contextlib.ExitStack()

    d_x = nc.dram_tensor("xb", [S, D], F32, kind="ExternalInput").ap()
    d_wq = nc.dram_tensor("wq", [D, D], F32, kind="ExternalInput").ap()
    d_wk = nc.dram_tensor("wk", [D, D], F32, kind="ExternalInput").ap()
    d_wv = nc.dram_tensor("wv", [D, D], F32, kind="ExternalInput").ap()
    d_wo = nc.dram_tensor("wo", [D, D], F32, kind="ExternalInput").ap()
    d_wdn = nc.dram_tensor("wdn", [D, DL], F32, kind="ExternalInput").ap()
    d_wuk = nc.dram_tensor("wuk", [DL, D], F32, kind="ExternalInput").ap()
    d_wuv = nc.dram_tensor("wuv", [DL, D], F32, kind="ExternalInput").ap()
    d_w1t = nc.dram_tensor("w1t", [D, DH], F32, kind="ExternalInput").ap()
    d_w2t = nc.dram_tensor("w2t", [DH, D], F32, kind="ExternalInput").ap()
    d_cw = nc.dram_tensor("cw", [128, 96], F32, kind="ExternalInput").ap()
    d_cb = nc.dram_tensor("cb", [128, 32], F32, kind="ExternalInput").ap()
    d_rows = nc.dram_tensor("rows3", [1, 3 * QW], F32, kind="ExternalInput").ap()
    d_rkf = nc.dram_tensor("rkf", [1, S - QW], F32, kind="ExternalInput").ap()
    d_dint = nc.dram_tensor("dint", [128, 128], F32, kind="ExternalInput").ap()
    d_eye = nc.dram_tensor("eye", [128, 128], BF16, kind="ExternalInput").ap()
    d_eyef = nc.dram_tensor("eyef", [128, 128], F32, kind="ExternalInput").ap()
    d_y = nc.dram_tensor("y", [QW, D], F32, kind="ExternalOutput").ap()
    d_dbg = nc.dram_tensor("dbg", [128, 16384], BF16, kind="ExternalOutput").ap()
    d_r1 = nc.dram_tensor("r1s", [QW, D], F32, kind="Internal").ap()

    s_sy = Sem(st, nc, "s_sy")
    s_gq = Sem(st, nc, "s_gq")
    s_pe = Sem(st, nc, "s_pe")
    s_dv = Sem(st, nc, "s_dv")
    s_ac = Sem(st, nc, "s_ac")
    s_st = Sem(st, nc, "s_st")
    kb = KB(nc, [s_sy, s_gq, s_pe, s_dv, s_ac])
    A = st.enter_context

    sb_eye = A(nc.sbuf_tensor("eye_s", [128, 128], BF16))
    sb_eyef = A(nc.sbuf_tensor("eyef_s", [128, 128], F32))
    sb_cw = A(nc.sbuf_tensor("cwt", [128, 96], F32))
    sb_cb = A(nc.sbuf_tensor("cbt", [128, 32], F32))
    sb_o64 = A(nc.sbuf_tensor("o64", [1, 64], F32))
    sb_o64b = A(nc.sbuf_tensor("o64b", [1, 64], BF16))
    sb_o1f = A(nc.sbuf_tensor("o1f", [1, 128], F32))
    sb_ocf = A(nc.sbuf_tensor("ocf", [128, 1], F32))
    sb_b4 = A(nc.sbuf_tensor("b4", [128, 1], F32))
    sb_beps = A(nc.sbuf_tensor("beps", [128, 1], F32))
    sb_stats = A(nc.sbuf_tensor("stats", [128, 24], F32))
    sb_wst = A(nc.sbuf_tensor("wstc", [128, 4], F32))
    sb_wstb = A(nc.sbuf_tensor("wstb", [128, 4], F32))
    sb_xt = [A(nc.sbuf_tensor(f"xt{i}", [128, D], F32)) for i in range(2)]
    sb_q = A(nc.sbuf_tensor("q_fm", [128, 8 * QW], BF16))

    kb.op("vector", lambda e: e.memset(sb_o64[:], 1.0))
    kb.op("vector", lambda e: e.memset(sb_o64b[:], 1.0))
    kb.op("vector", lambda e: e.memset(sb_o1f[:], 1.0))
    kb.op("vector", lambda e: e.memset(sb_ocf[:], 1.0))
    kb.op("vector", lambda e: e.memset(sb_b4[:], EBIAS))
    kb.op("vector", lambda e: e.memset(sb_beps[:], EPS))
    kb.op("vector", lambda e: e.memset(sb_wst[:], 0.0), inc=s_dv)
    ev_memset = s_dv.now

    ln_e2_prev = {}

    def ln_cols(src, n, ps_sq, c0, P=128, wv=(), wa=()):
        stc = sb_stats[0:P, :]
        kb.op("vector", lambda e: e.reduce_sum(stc[:, c0 + 3:c0 + 4], src, axis=AX.X),
              waits=list(wv), inc=s_dv)
        ev_red = s_dv.now
        kb.op("scalar", lambda e: e.activation(ps_sq[0:P, 0:n], src, AF.Square,
                                               accum_out=stc[:, c0 + 4:c0 + 5]),
              waits=list(wa) + _w([ln_e2_prev.get(c0)]), inc=s_ac)
        ev_sq = s_ac.now
        kb.op("vector", lambda e: e.tensor_scalar_mul(
            stc[:, c0:c0 + 1], stc[:, c0 + 3:c0 + 4], 1.0 / n), waits=[ev_red],
            inc=s_dv)
        ev_m = s_dv.now
        kb.op("vector", lambda e: e.tensor_scalar_mul(
            stc[:, c0 + 5:c0 + 6], stc[:, c0 + 4:c0 + 5], 1.0 / n), waits=[ev_sq],
            inc=s_dv)
        ln_e2_prev[c0] = s_dv.now
        ev_e2 = s_dv.now
        kb.op("vector", lambda e: e.tensor_mul(
            stc[:, c0 + 6:c0 + 7], stc[:, c0:c0 + 1], stc[:, c0:c0 + 1]),
            waits=[ev_m], inc=s_dv)
        ev_mm = s_dv.now
        kb.op("vector", lambda e: e.tensor_sub(
            stc[:, c0 + 5:c0 + 6], stc[:, c0 + 5:c0 + 6], stc[:, c0 + 6:c0 + 7]),
            waits=[ev_e2, ev_mm], inc=s_dv)
        ev_var = s_dv.now
        kb.op("scalar", lambda e: e.activation(
            stc[:, c0 + 6:c0 + 7], stc[:, c0 + 5:c0 + 6], AF.Ln,
            bias=sb_beps[0:P, 0:1]), waits=[ev_var], inc=s_ac)
        ev_ln = s_ac.now
        kb.op("scalar", lambda e: e.activation(
            stc[:, c0 + 1:c0 + 2], stc[:, c0 + 6:c0 + 7], AF.Exp, scale=-0.5),
            waits=[ev_ln], inc=s_ac)
        ev_rstd = s_ac.now
        kb.op("vector", lambda e: e.tensor_scalar(
            stc[:, c0 + 2:c0 + 3], stc[:, c0:c0 + 1], stc[:, c0 + 1:c0 + 2], -1.0,
            ALU.mult, ALU.mult), waits=[ev_rstd], inc=s_dv)
        return s_dv.now

    attn_es = contextlib.ExitStack()
    Aa = attn_es.enter_context
    sb_fp = Aa(nc.sbuf_tensor("fp_sb", [128, QW], BF16))
    sb_fm = Aa(nc.sbuf_tensor("fm_sb", [128, QW], BF16))
    sb_fqf = Aa(nc.sbuf_tensor("fqf", [128, QW], BF16))
    sb_cbig = Aa(nc.sbuf_tensor("cbig", [128, 128], BF16))
    sb_lat = Aa(nc.sbuf_tensor("lat", [128, 2 * S], BF16))
    sb_wu = Aa(nc.sbuf_tensor("wu_sb", [128, 4096], BF16))
    p2a = contextlib.ExitStack()
    sb_latA = p2a.enter_context(nc.sbuf_tensor("latA", [128, 2 * S], BF16))
    sb_latB = p2a.enter_context(nc.sbuf_tensor("latB", [128, 2 * QW], BF16))

    ramp_es = contextlib.ExitStack()
    sb_rows = ramp_es.enter_context(nc.sbuf_tensor("rows", [1, 3 * QW], F32))
    sb_rkf = ramp_es.enter_context(nc.sbuf_tensor("rkfs", [1, S - QW], F32))
    p01 = contextlib.ExitStack()
    Aw = p01.enter_context
    sb_wcomb = Aw(nc.sbuf_tensor("wcomb", [128, 8 * DL], BF16))
    sb_wdn = Aw(nc.sbuf_tensor("wdns", [128, 8 * DL], BF16))
    sb_wsa = [Aw(nc.sbuf_tensor(f"wsa{i}", [128, 1024], F32)) for i in range(2)]
    sb_wqw = Aw(nc.sbuf_tensor("wq_sb", [128, 8 * D], BF16))
    sb_xn = [Aw(nc.sbuf_tensor(f"xn{i}", [128, D], BF16)) for i in range(2)]
    sb_xnf = [Aw(nc.sbuf_tensor(f"xnf{i}", [128, D], BF16)) for i in range(2)]
    wprep = contextlib.ExitStack()
    sb_dintt = wprep.enter_context(nc.sbuf_tensor("dintt", [128, 128], F32))
    sb_wkv = wprep.enter_context(nc.sbuf_tensor("wkv", [128, 8 * D], BF16))
    sb_wkvc = wprep.enter_context(nc.sbuf_tensor("wkvc", [128, D], BF16))

    psw = contextlib.ExitStack()
    ps_tb = psw.enter_context(nc.psum_tensor("ps_tb", [128, 1024], BF16))
    ps_row = psw.enter_context(nc.psum_tensor("ps_row", [128, 512], F32))
    ps_sq = psw.enter_context(nc.psum_tensor("ps_sq", [128, 1024], F32))
    pswc = contextlib.ExitStack()
    ps_wc = pswc.enter_context(nc.psum_tensor("ps_wc", [128, 8 * DL], F32))

    for dst, src in ((sb_eye[:], d_eye), (sb_eyef[:], d_eyef), (sb_cw[:], d_cw),
                     (sb_cb[:], d_cb), (sb_dintt[:], d_dint), (sb_rows[:], d_rows),
                     (sb_rkf[:], d_rkf)):
        kb.op("sync", lambda e, d=dst, s=src: e.dma_start(out=d, in_=s), inc=s_sy, inck=16)
    ev_consts = s_sy.now

    def cast3(dst2, src2, cols):
        s3 = src2.rearrange("(c p) j -> p c j", p=128)
        d3 = dst2.rearrange("p (c j) -> p c j", j=cols)
        return d3, s3
    for dst, src, cols in ((sb_wqw[:], d_wq, D), (sb_wdn[:], d_wdn, DL),
                           (sb_wu[:, 0:2048], d_wuk, D),
                           (sb_wu[:, 2048:4096], d_wuv, D)):
        kb.op("gpsimd", lambda e, d=dst, s=src, c=cols: e.dma_start(
            out=cast3(d, s, c)[0], in_=cast3(d, s, c)[1]), inc=s_gq, inck=16)
    ev_gw = s_gq.now

    if PH == 11:
        kb.wait("sync", (s_st, s_st.n))
        with nc.Block() as block:
            kb.emit(block)
        return nc

    ev_kv_dv = []
    for c in range(8):
        kb.op("sync", lambda e, c=c: e.dma_start(
            out=sb_wsa[0][:], in_=d_wk[c * 128:(c + 1) * 128, :]),
            waits=_w([ev_kv_dv[-1] if ev_kv_dv else None]), inc=s_sy, inck=16)
        kb.op("sync", lambda e, c=c: e.dma_start(
            out=sb_wsa[1][:], in_=d_wv[c * 128:(c + 1) * 128, :]),
            inc=s_sy, inck=16)
        kb.op("vector", lambda e, c=c: e.tensor_tensor(
            sb_wkv[:, c * D:(c + 1) * D], sb_wsa[0][:],
            sb_wsa[1][:], ALU.add), waits=[s_sy.now], inc=s_dv)
        ev_kv_dv.append(s_dv.now)
    ev_wkv = s_dv.now

    ev_wm = [ev_wkv]
    mjobs = []
    for c in range(8):
        for qq in range(4):
            mjobs.append((d_w1t[c * 128:(c + 1) * 128, qq * 1024:(qq + 1) * 1024], 0))
    for c in range(32):
        mjobs.append((d_w2t[c * 128:(c + 1) * 128, :], 1))
    for i, (s1, lane) in enumerate(mjobs):
        b = i % 2
        war = _w([ev_wm[-2] if len(ev_wm) >= 2 else None, ev_wkv])
        kb.op("sync", lambda e, b=b, s=s1: e.dma_start(out=sb_wsa[b][:], in_=s),
              waits=war, inc=s_sy, inck=16)
        ev_ld = s_sy.now
        kb.op("vector", lambda e, b=b: e.reduce_sum(
            sb_stats[:, 7:8], sb_wsa[b][:], axis=AX.X, apply_absolute_value=True),
            waits=[ev_ld, ev_memset] + _w([ev_wm[-1] if len(ev_wm) > 1 else None]),
            inc=s_dv)
        kb.op("vector", lambda e, lane=lane: e.tensor_add(
            sb_wst[:, lane:lane + 1], sb_wst[:, lane:lane + 1], sb_stats[:, 7:8]),
            waits=[(s_dv, s_dv.n)], inc=s_dv)
        ev_wm.append(s_dv.now)
    ev_wmean = s_dv.now
    kb.op("tensor", lambda e: e.matmul(ps_row[0:1, 0:2], sb_ocf[:, 0:1],
                                       sb_wst[:, 0:2], start=True, stop=True),
          waits=[ev_wmean], inc=s_pe)
    kb.op("vector", lambda e: e.tensor_scalar_mul(
        sb_wst[0:1, 2:3], ps_row[0:1, 0:1], 1.0 / (D * DH)), waits=[s_pe.now],
        inc=s_dv)
    kb.op("vector", lambda e: e.tensor_scalar_mul(
        sb_wst[0:1, 3:4], ps_row[0:1, 1:2], 1.0 / (D * DH)), inc=s_dv)
    kb.op("vector", lambda e: e.reciprocal(sb_stats[0:1, 8:9], sb_wst[0:1, 2:3]),
          waits=[(s_dv, s_dv.n)], inc=s_dv)
    kb.op("vector", lambda e: e.reciprocal(sb_stats[0:1, 9:10], sb_wst[0:1, 3:4]),
          inc=s_dv)
    kb.op("vector", lambda e: e.tensor_copy(sb_stats[0:1, 10:11], sb_wst[0:1, 2:3]),
          inc=s_dv)
    kb.op("vector", lambda e: e.tensor_copy(sb_stats[0:1, 11:12], sb_wst[0:1, 3:4]),
          waits=[(s_dv, s_dv.n)], inc=s_dv)
    kb.op("tensor", lambda e: e.matmul(ps_row[:, 4:8], sb_o1f[:],
                                       sb_stats[0:1, 8:12], start=True, stop=True),
          waits=[s_dv.now], inc=s_pe)
    kb.op("vector", lambda e: e.tensor_copy(sb_wstb[:, 0:4], ps_row[:, 4:8]),
          waits=[s_pe.now], inc=s_dv)
    ev_wstb = s_dv.now

    if PH == 12:
        kb.wait("sync", (s_st, s_st.n))
        with nc.Block() as block:
            kb.emit(block)
        return nc

    kb.wait("tensor", ev_consts, ev_wkv, ev_gw)
    ev_ch = None
    for ic in range(8):
        for jc in range(8):
            kb.op("tensor", lambda e, jc=jc, ic=ic: e.transpose(
                ps_tb[:, jc * 128:(jc + 1) * 128],
                sb_wkv[:, ic * D + jc * 128: ic * D + (jc + 1) * 128], sb_eye[:]),
                waits=_w([ev_ch]) if jc == 0 else ())
        kb.op("tensor", lambda e: e.drain(), inc=s_pe)
        kb.op("vector", lambda e: e.tensor_copy(sb_wkvc[:], ps_tb[:]),
              waits=[s_pe.now], inc=s_dv)
        ev_ch = s_dv.now
        kb.wait("tensor", ev_ch)
        for jc in range(8):
            kb.op("tensor", lambda e, ic=ic, jc=jc: e.matmul(
                ps_wc[:, ic * DL:(ic + 1) * DL],
                sb_wkvc[:, jc * 128:(jc + 1) * 128],
                sb_wdn[:, jc * DL:(jc + 1) * DL], start=(jc == 0), stop=(jc == 7)))
    kb.op("tensor", lambda e: e.drain(), inc=s_pe)
    kb.op("vector", lambda e: e.tensor_copy(sb_wcomb[:], ps_wc[:]),
          waits=[s_pe.now], inc=s_dv)
    ev_wcomb = s_dv.now

    if PH == 13:
        kb.wait("sync", (s_st, s_st.n))
        with nc.Block() as block:
            kb.emit(block)
        return nc

    kb.wait("tensor", ev_consts)
    last_f = [None]

    def ftab(dst, width, srcrow, off):
        for t0 in range(0, width, 512):
            w = min(512, width - t0)
            kb.op("tensor", lambda e, t0=t0, w=w, sr=srcrow, o=off: e.matmul(
                ps_row[:, 0:w], sb_o1f[:], sr[0:1, o + t0:o + t0 + w],
                start=True, stop=True), waits=_w([last_f[0]]), inc=s_pe)
            kb.op("scalar", lambda e, t0=t0, w=w, d=dst: e.activation(
                d[:, t0:t0 + w], ps_row[:, 0:w], AF.Exp, scale=ADEC),
                waits=[s_pe.now], inc=s_ac)
            last_f[0] = s_ac.now
    ftab(sb_fp, QW, sb_rows, 0)
    ftab(sb_fm, QW, sb_rows, QW)
    ftab(sb_fqf, QW, sb_rows, 2 * QW)
    ev_f = last_f[0]
    kb.op("scalar", lambda e: e.activation(sb_cbig[:], sb_dintt[:], AF.Exp,
                                           scale=-2.0 * ADEC),
          waits=[ev_consts], inc=s_ac)
    kb.op("vector", lambda e: e.tensor_scalar_min(sb_cbig[:], sb_cbig[:], 1.0),
          waits=[s_ac.now], inc=s_dv)
    ev_cbig = s_dv.now
    pswc.close()
    wprep.close()

    if PH == 14:
        kb.wait("sync", (s_st, s_st.n))
        with nc.Block() as block:
            kb.emit(block)
        return nc
    psl_es = contextlib.ExitStack()
    ps_l = psl_es.enter_context(nc.psum_tensor("ps_l", [128, 128], F32))
    ps_qq = psl_es.enter_context(nc.psum_tensor("ps_qq", [128, 128], F32))

    ev_xt_rd = [None, None]
    ev_xn_rd = [None, None]
    ev_xnf_rd = [None, None]
    ev_tb_rd = None
    ev_psl_rd = [None, None]
    ev_psq_rd = [None, None]
    kb.wait("tensor", ev_wcomb, ev_gw)
    for t in range(32):
        b = t % 2
        kb.op("sync", lambda e, t=t, b=b: e.dma_start(
            out=sb_xt[b][:], in_=d_x[t * 128:(t + 1) * 128, :]),
            waits=_w([ev_xt_rd[b]]), inc=s_sy, inck=16)
        ev_x = s_sy.now
        evm = ln_cols(sb_xt[b][:], D, ps_sq, 0, wv=[ev_x], wa=[ev_x])
        kb.op("vector", lambda e, b=b: e.tensor_scalar(
            sb_xn[b][:], sb_xt[b][:], sb_stats[:, 0:1], sb_stats[:, 1:2],
            ALU.subtract, ALU.mult), waits=[evm] + _w([ev_xn_rd[b]]), inc=s_dv)
        ev_xn = s_dv.now
        ev_xt_rd[b] = ev_xn
        for c in range(8):
            kb.op("tensor", lambda e, b=b, c=c: e.transpose(
                ps_tb[:, c * 128:(c + 1) * 128], sb_xn[b][:, c * 128:(c + 1) * 128],
                sb_eye[:]),
                waits=([ev_xn] + _w([ev_tb_rd])) if c == 0 else ())
        kb.op("tensor", lambda e: e.drain(), inc=s_pe)
        kb.op("vector", lambda e, b=b: e.tensor_copy(sb_xnf[b][:], ps_tb[:]),
              waits=[s_pe.now] + _w([ev_xnf_rd[b]]), inc=s_dv)
        ev_xnf = s_dv.now
        ev_xn_rd[b] = ev_xnf
        ev_tb_rd = ev_xnf
        for oc in range(2):
            for kc in range(8):
                kb.op("tensor", lambda e, oc=oc, kc=kc, b=b: e.matmul(
                    ps_l[:], sb_wcomb[:, kc * DL + oc * 128: kc * DL + (oc + 1) * 128],
                    sb_xnf[b][:, kc * 128:(kc + 1) * 128],
                    start=(kc == 0), stop=(kc == 7)),
                    waits=([ev_xnf] + _w([ev_psl_rd[0]])) if kc == 0 else ())
            kb.op("tensor", lambda e: e.drain(), inc=s_pe)
            kb.op("vector", lambda e, oc=oc, t=t: e.tensor_copy(
                sb_lat[:, oc * S + t * 128: oc * S + (t + 1) * 128], ps_l[:]),
                waits=[s_pe.now], inc=s_dv)
            ev_psl_rd[0] = s_dv.now
            kb.wait("tensor", ev_psl_rd[0])
        if t < NQT:
            for oc in range(8):
                for kc in range(8):
                    kb.op("tensor", lambda e, oc=oc, kc=kc, b=b: e.matmul(
                        ps_qq[:], sb_wqw[:, kc * D + oc * 128: kc * D + (oc + 1) * 128],
                        sb_xnf[b][:, kc * 128:(kc + 1) * 128],
                        start=(kc == 0), stop=(kc == 7)),
                        waits=_w([ev_psq_rd[0]]) if kc == 0 else ())
                kb.op("tensor", lambda e: e.drain(), inc=s_pe)
                kb.op("vector", lambda e, oc=oc, t=t: e.tensor_copy(
                    sb_q[:, oc * QW + t * 128: oc * QW + (t + 1) * 128], ps_qq[:]),
                    waits=[s_pe.now], inc=s_dv)
                ev_psq_rd[0] = s_dv.now
                kb.wait("tensor", ev_psq_rd[0])
        kb.op("vector", lambda e: e.engine_nop(), inc=s_dv)
        ev_xnf_rd[b] = s_dv.now
        if PH == 16 and t == 0:
            kb.op("sync", lambda e: e.dma_start(out=d_dbg[:, 0:1024], in_=sb_xn[0][:]),
                  waits=[(s_dv, s_dv.n)], inc=s_st, inck=16)
            kb.op("sync", lambda e: e.dma_start(out=d_dbg[:, 1024:2048], in_=sb_xnf[0][:]),
                  inc=s_st, inck=16)
            kb.op("sync", lambda e: e.dma_start(out=d_dbg[:, 2048:2176], in_=sb_lat[:, 0:128]),
                  inc=s_st, inck=16)
            kb.op("sync", lambda e: e.dma_start(out=d_dbg[:, 2176:2304], in_=sb_lat[:, S:S + 128]),
                  inc=s_st, inck=16)
            kb.wait("sync", (s_st, s_st.n))
            with nc.Block() as block:
                kb.emit(block)
            return nc

    psl_es.close()
    psw.close()
    p01.close()

    if PH <= 1:
        kb.op("sync", lambda e: e.dma_start(out=d_dbg[:, 0:2048], in_=sb_wcomb[:]),
              waits=[(s_dv, s_dv.n)], inc=s_st, inck=16)
        kb.op("sync", lambda e: e.dma_start(out=d_dbg[:, 2048:6144], in_=sb_lat[:, 0:4096]),
              inc=s_st, inck=16)
        kb.op("sync", lambda e: e.dma_start(out=d_dbg[:, 6144:10240], in_=sb_q[:, 0:4096]),
              inc=s_st, inck=16)
        kb.op("sync", lambda e: e.dma_start(out=d_dbg[:, 10240:12416], in_=sb_fp[:, 0:QW]),
              inc=s_st, inck=16)
        kb.op("sync", lambda e: e.dma_start(out=d_dbg[:, 12416:12544], in_=sb_cbig[:]),
              inc=s_st, inck=16)
        kb.op("sync", lambda e: e.dma_start(out=d_dbg[:, 12544:14592], in_=sb_wu[:, 0:2048]),
              inc=s_st, inck=16)
        kb.wait("sync", (s_st, s_st.n))
        with nc.Block() as block:
            kb.emit(block)
        return nc

    kb.barrier_all()
    fa = contextlib.ExitStack()
    sb_fa = fa.enter_context(nc.sbuf_tensor("fab", [128, S], F32))
    sb_fb = fa.enter_context(nc.sbuf_tensor("fbb", [128, QW], F32))
    psf = contextlib.ExitStack()
    ps_fw = psf.enter_context(nc.psum_tensor("ps_fw", [128, 512], F32))
    last_f[0] = None
    for (dst, off, width, srcrow, so) in (
            (sb_fa, 0, QW, sb_rows, 0),
            (sb_fa, QW, S - QW, sb_rkf, 0),
            (sb_fb, 0, QW, sb_rows, QW)):
        for t0 in range(0, width, 512):
            w = min(512, width - t0)
            kb.op("tensor", lambda e, t0=t0, w=w, sr=srcrow, so=so: e.matmul(
                ps_fw[:, 0:w], sb_o1f[:], sr[0:1, so + t0:so + t0 + w],
                start=True, stop=True), waits=_w([last_f[0]]), inc=s_pe)
            kb.op("scalar", lambda e, d=dst, off=off, t0=t0, w=w: e.activation(
                d[:, off + t0:off + t0 + w], ps_fw[:, 0:w], AF.Exp, scale=ADEC),
                waits=[s_pe.now], inc=s_ac)
            last_f[0] = s_ac.now
    ev_fab = last_f[0]
    for c2 in range(2):
        for t0 in range(0, S, 2048):
            kb.op("vector", lambda e, c2=c2, t0=t0: e.tensor_tensor(
                sb_latA[:, c2 * S + t0: c2 * S + t0 + 2048],
                sb_lat[:, c2 * S + t0: c2 * S + t0 + 2048],
                sb_fa[:, t0:t0 + 2048], ALU.mult),
                waits=[ev_fab] if (c2 == 0 and t0 == 0) else ())
        kb.op("vector", lambda e, c2=c2: e.tensor_tensor(
            sb_latB[:, c2 * QW:(c2 + 1) * QW], sb_lat[:, c2 * S: c2 * S + QW],
            sb_fb[:, 0:QW], ALU.mult))
    kb.op("vector", lambda e: e.engine_nop(), inc=s_dv)
    ev_latAB = s_dv.now
    psf.close()
    fa.close()
    ramp_es.close()
    ps2 = contextlib.ExitStack()
    ps_sc = [ps2.enter_context(nc.psum_tensor(f"ps_sc{i}", [128, 1024], F32))
             for i in range(2)]
    ps_pv = ps2.enter_context(nc.psum_tensor("ps_pv", [128, QMAIN], F32))
    cat_es = contextlib.ExitStack()
    sb_cat = cat_es.enter_context(nc.sbuf_tensor("cat", [128, 8 * QW], BF16))
    p2 = contextlib.ExitStack()
    A2 = p2.enter_context
    sb_v = A2(nc.sbuf_tensor("v_sb", [128, 32 * 520], BF16))
    sb_ka = A2(nc.sbuf_tensor("ka_sb", [64, S], BF16))
    sb_kb = A2(nc.sbuf_tensor("kb_sb", [64, QW], BF16))
    sb_qvL = A2(nc.sbuf_tensor("qvL", [64, QW], BF16))
    sb_qvR = A2(nc.sbuf_tensor("qvR", [64, QW], BF16))
    sb_qvf = A2(nc.sbuf_tensor("qvf", [64, QW], BF16))
    sb_pt = [A2(nc.sbuf_tensor(f"pt{i}", [128, 1024], BF16)) for i in range(2)]
    sb_un = A2(nc.sbuf_tensor("unorm", [128, 1024], BF16))
    sb_nrr = A2(nc.sbuf_tensor("nrowr", [1, 1024], F32))


    if PH == 21:
        kb.wait("sync", (s_st, s_st.n))
        with nc.Block() as block:
            kb.emit(block)
        return nc
    kb.wait("tensor", ev_latAB, ev_f, ev_cbig)
    pvfree = [None]
    ev_sc_rd = [None, None]
    ev_pt_rd = [None, None]
    prev_tqk = [None]

    def kslice(h, kc2):
        return sb_wu[:, kc2 * 1024 + h * 64: kc2 * 1024 + h * 64 + 64]

    def kbuild(h):
        evs = [pvfree[0]]
        for (c0, w) in ((0, 512), (512, 512), (1024, 512), (1536, 512)):
            for kc2 in range(2):
                kb.op("tensor", lambda e, h=h, kc2=kc2, c0=c0, w=w: e.matmul(
                    ps_pv[0:64, c0:c0 + w], kslice(h, kc2),
                    sb_latA[:, kc2 * S + c0: kc2 * S + c0 + w],
                    start=(kc2 == 0), stop=(kc2 == 1)),
                    waits=_w([evs[-1]]) if (c0 == 0 and kc2 == 0) else ())
                kb.op("tensor", lambda e, h=h, kc2=kc2, c0=c0, w=w: e.matmul(
                    ps_pv[64:128, c0:c0 + w], kslice(h, kc2),
                    sb_latB[:, kc2 * QW + c0: kc2 * QW + c0 + w],
                    start=(kc2 == 0), stop=(kc2 == 1), tile_position=(0, 64)))
        kb.op("tensor", lambda e: e.drain(), inc=s_pe)
        kb.op("vector", lambda e: e.tensor_copy(sb_ka[:, 0:2048], ps_pv[0:64, 0:2048]),
              waits=[s_pe.now])
        kb.op("vector", lambda e: e.tensor_copy(sb_kb[:, 0:2048], ps_pv[64:128, 0:2048]),
              inc=s_dv)
        evs.append(s_dv.now)
        for kc2 in range(2):
            kb.op("tensor", lambda e, h=h, kc2=kc2: e.matmul(
                ps_pv[0:64, 0:128], kslice(h, kc2),
                sb_latA[:, kc2 * S + 2048: kc2 * S + 2176],
                start=(kc2 == 0), stop=(kc2 == 1)),
                waits=[evs[-1]] if kc2 == 0 else ())
            kb.op("tensor", lambda e, h=h, kc2=kc2: e.matmul(
                ps_pv[64:128, 0:128], kslice(h, kc2),
                sb_latB[:, kc2 * QW + 2048: kc2 * QW + 2176],
                start=(kc2 == 0), stop=(kc2 == 1), tile_position=(0, 64)))
        kb.op("tensor", lambda e: e.drain(), inc=s_pe)
        kb.op("vector", lambda e: e.tensor_copy(sb_ka[:, 2048:2176], ps_pv[0:64, 0:128]),
              waits=[s_pe.now])
        kb.op("vector", lambda e: e.tensor_copy(sb_kb[:, 2048:2176], ps_pv[64:128, 0:128]),
              inc=s_dv)
        evs.append(s_dv.now)
        for (c0, w) in ((0, 512), (512, 512), (1024, 512), (1536, 384)):
            for kc2 in range(2):
                kb.op("tensor", lambda e, h=h, kc2=kc2, c0=c0, w=w: e.matmul(
                    ps_pv[0:64, c0:c0 + w], kslice(h, kc2),
                    sb_latA[:, kc2 * S + QW + c0: kc2 * S + QW + c0 + w],
                    start=(kc2 == 0), stop=(kc2 == 1)),
                    waits=[evs[-1]] if (c0 == 0 and kc2 == 0) else ())
        kb.op("tensor", lambda e: e.drain(), inc=s_pe)
        kb.op("vector", lambda e: e.tensor_copy(
            sb_ka[:, QW:S], ps_pv[0:64, 0:S - QW]), waits=[s_pe.now], inc=s_dv)
        return s_dv.now

    def scores_pieces(kc, qh):
        base = qh * 1024
        if kc >= NQT:
            return [(0, 512, 3), (512, 512, 3)]
        dlo = kc * 128
        out = []
        for s0 in (0, 512):
            gs, ge = base + s0, base + s0 + 512
            r1 = min(ge, dlo)
            if r1 > gs:
                out.append((gs - base, r1 - gs, 1))
            d0, d1 = max(gs, dlo), min(ge, dlo + 128)
            if d1 > d0:
                out.append((d0 - base, d1 - d0, 2))
            l0 = max(gs, dlo + 128)
            if ge > l0:
                out.append((l0 - base, ge - l0, 0))
        return out

    for hh in range(2):
        kb.op("vector", lambda e: e.memset(
            sb_v[:].rearrange("p (k s) -> p k s", s=65)[:, :, 64:65], 1.0),
            waits=_w([pvfree[0]]), inc=s_dv)
        ev_vev = [s_dv.now, None, None, None]
        kb.wait("tensor", s_dv.now)
        for kt in range(32):
            sl = kt % 4
            for kc2 in range(2):
                kb.op("tensor", lambda e, kt=kt, kc2=kc2, sl=sl, hh=hh: e.matmul(
                    ps_pv[:, sl * 512:(sl + 1) * 512],
                    sb_lat[:, kc2 * S + kt * 128: kc2 * S + (kt + 1) * 128],
                    sb_wu[:, 2048 + kc2 * 1024 + hh * 512: 2048 + kc2 * 1024 + hh * 512 + 512],
                    start=(kc2 == 0), stop=(kc2 == 1)),
                    waits=_w([ev_vev[sl]]) if kc2 == 0 else ())
            kb.op("tensor", lambda e: e.drain(), inc=s_pe)
            kb.op("vector", lambda e, kt=kt, sl=sl: e.tensor_copy(
                sb_v[:, kt * 520:(kt + 1) * 520].rearrange(
                    "p (h s) -> p h s", s=65)[:, :, 0:64],
                ps_pv[:, sl * 512:(sl + 1) * 512].rearrange(
                    "p (h s) -> p h s", s=64)),
                waits=[s_pe.now], inc=s_dv)
            ev_vev[sl] = s_dv.now
        kb.op("vector", lambda e: e.engine_nop(), inc=s_dv)
        pvfree[0] = s_dv.now
        if PH == 22 and hh == 0:
            kb.wait("sync", (s_st, s_st.n))
            with nc.Block() as block:
                kb.emit(block)
            return nc

        for h8 in range(8):
            h = hh * 8 + h8
            hp = (h % 2) * 64
            blk = (h // 2) * QW
            ev_k = kbuild(h)
            pvfree[0] = ev_k
            if PH == 23 and h == 0:
                kb.wait("sync", (s_st, s_st.n))
                with nc.Block() as block:
                    kb.emit(block)
                return nc
            kb.op("vector", lambda e, hp=hp, blk=blk: e.scalar_tensor_tensor(
                sb_qvL[:, :], sb_q[hp:hp + 64, blk:blk + QW], SCALE,
                sb_fm[hp:hp + 64, :], ALU.mult, ALU.mult), waits=_w([prev_tqk[0]]))
            kb.op("vector", lambda e, hp=hp, blk=blk: e.scalar_tensor_tensor(
                sb_qvR[:, :], sb_q[hp:hp + 64, blk:blk + QW], SCALE,
                sb_fp[hp:hp + 64, :], ALU.mult, ALU.mult))
            kb.op("vector", lambda e, hp=hp, blk=blk: e.scalar_tensor_tensor(
                sb_qvf[:, :], sb_q[hp:hp + 64, blk:blk + QW], SCALE,
                sb_fqf[hp:hp + 64, :], ALU.mult, ALU.mult), inc=s_dv)
            ev_qv = s_dv.now
            kb.wait("tensor", ev_qv)

            for qh in range(2):
                for kc in range(32):
                    par = kc % 2
                    pieces = scores_pieces(kc, qh)
                    first = True
                    for (c0, w, side) in pieces:
                        g0 = qh * 1024 + c0
                        if side == 3:
                            lh = sb_ka[:, kc * 128:(kc + 1) * 128]
                            rh = sb_qvf[:, g0:g0 + w]
                        elif side == 1:
                            lh = sb_kb[:, kc * 128:(kc + 1) * 128]
                            rh = sb_qvR[:, g0:g0 + w]
                        else:
                            lh = sb_ka[:, kc * 128:(kc + 1) * 128]
                            rh = sb_qvL[:, g0:g0 + w]
                        kb.op("tensor", lambda e, lh=lh, rh=rh, c0=c0, w=w, par=par: e.matmul(
                            ps_sc[par][:, c0:c0 + w], lh, rh, start=True, stop=True),
                            waits=_w([ev_sc_rd[par]]) if first else ())
                        first = False
                    kb.op("tensor", lambda e: e.drain(), inc=s_pe)
                    ev_qk = s_pe.now
                    diag = [] if KNODIAG else [p for p in pieces if p[2] == 2]
                    if diag:
                        c0 = diag[0][0]
                        kb.op("vector", lambda e, c0=c0, par=par: e.tensor_tensor(
                            ps_sc[par][:, c0:c0 + 128], ps_sc[par][:, c0:c0 + 128],
                            sb_cbig[:, 0:128], ALU.mult), waits=[ev_qk], inc=s_dv)
                        exp_wait = [s_dv.now]
                    else:
                        exp_wait = [ev_qk]
                    if KNOEXP:
                        kb.op("scalar", lambda e, par=par: e.activation(
                            sb_pt[par][:], sb_qvL[0:64, 0:1024], AF.Exp,
                            bias=sb_b4[:, 0:1]),
                            waits=exp_wait + _w([ev_pt_rd[par]]), inc=s_ac)
                    else:
                        kb.op("scalar", lambda e, par=par: e.activation(
                            sb_pt[par][:], ps_sc[par][:], AF.Exp, bias=sb_b4[:, 0:1]),
                            waits=exp_wait + _w([ev_pt_rd[par]]), inc=s_ac)
                    ev_sc_rd[par] = s_ac.now
                    ev_exp = s_ac.now
                    if KNOPV:
                        kb.op("tensor", lambda e: e.drain(),
                              waits=[ev_exp], inc=s_pe)
                        ev_pt_rd[par] = s_pe.now
                        continue
                    for n0 in range(0, 1024, 512):
                        kb.op("tensor", lambda e, h8=h8, kc=kc, n0=n0, par=par, qh=qh: e.matmul(
                            ps_pv[0:65, qh * 1024 + n0: qh * 1024 + n0 + 512],
                            sb_v[:, kc * 520 + h8 * 65: kc * 520 + h8 * 65 + 65],
                            sb_pt[par][:, n0:n0 + 512],
                            start=(kc == 0), stop=(kc == 31)),
                            waits=([ev_exp] + _w([pvfree[0] if (kc == 0 and qh == 0) else None]))
                            if n0 == 0 else ())
                    kb.op("tensor", lambda e: e.drain(), inc=s_pe)
                    ev_pt_rd[par] = s_pe.now
            ev_pv_done = s_pe.now
            if PH == 24 and h == 0:
                kb.wait("tensor", ev_pv_done)
                kb.wait("sync", (s_st, s_st.n))
                with nc.Block() as block:
                    kb.emit(block)
                return nc

            for half in range(2):
                o = half * 1024
                kb.op("vector", lambda e, o=o: e.tensor_copy(
                    sb_un[0:65, :], ps_pv[0:65, o:o + 1024]),
                    waits=[ev_pv_done], inc=s_dv)
                ev_ev = s_dv.now
                kb.op("vector", lambda e, o=o: e.reciprocal(
                    sb_nrr[:], ps_pv[64:65, o:o + 1024]), inc=s_dv)
                ev_rec = s_dv.now
                for n0 in range(0, 1024, 512):
                    kb.op("tensor", lambda e, o=o, n0=n0: e.matmul(
                        ps_pv[0:64, o + n0:o + n0 + 512], sb_o64[:],
                        sb_nrr[0:1, n0:n0 + 512], start=True, stop=True),
                        waits=[ev_ev, ev_rec] if n0 == 0 else ())
                kb.op("tensor", lambda e: e.drain(), inc=s_pe)
                ev_bc = s_pe.now
                for n0 in range(0, 1024, 512):
                    kb.op("vector", lambda e, o=o, n0=n0, hp=hp, blk=blk: e.tensor_tensor(
                        sb_cat[hp:hp + 64, blk + o + n0: blk + o + n0 + 512],
                        sb_un[0:64, n0:n0 + 512], ps_pv[0:64, o + n0:o + n0 + 512],
                        ALU.mult), waits=[ev_bc] if n0 == 0 else ())
                kb.op("vector", lambda e: e.engine_nop(), inc=s_dv)
            pvfree[0] = s_dv.now

            ev_tpv = None
            for g in range(4):
                par = g % 2
                for i8 in range(8):
                    kc = g * 8 + i8
                    slot = i8 * 128
                    if kc >= NQT:
                        rh = sb_qvf[:, QMAIN:QW]
                    else:
                        rh = sb_qvL[:, QMAIN:QW]
                    lh = sb_ka[:, kc * 128:(kc + 1) * 128]
                    kb.op("tensor", lambda e, lh=lh, rh=rh, slot=slot, par=par: e.matmul(
                        ps_sc[par][:, slot:slot + 128], lh, rh, start=True, stop=True),
                        waits=_w([ev_sc_rd[par]]) if i8 == 0 else ())
                kb.op("tensor", lambda e: e.drain(), inc=s_pe)
                ev_tqk = s_pe.now
                prev_tqk[0] = ev_tqk
                if g == 2:
                    kb.op("vector", lambda e, par=par: e.tensor_tensor(
                        ps_sc[par][:, 0:128], ps_sc[par][:, 0:128],
                        sb_cbig[:, 0:128], ALU.mult), waits=[ev_tqk], inc=s_dv)
                    twait = [s_dv.now]
                else:
                    twait = [ev_tqk]
                kb.op("scalar", lambda e, par=par: e.activation(
                    sb_pt[par][:], ps_sc[par][:], AF.Exp, bias=sb_b4[:, 0:1]),
                    waits=twait + _w([ev_pt_rd[par]]), inc=s_ac)
                ev_sc_rd[par] = s_ac.now
                ev_texp = s_ac.now
                for i8 in range(8):
                    kc = g * 8 + i8
                    kb.op("tensor", lambda e, h8=h8, kc=kc, par=par, i8=i8: e.matmul(
                        ps_pv[0:65, 0:128],
                        sb_v[:, kc * 520 + h8 * 65: kc * 520 + h8 * 65 + 65],
                        sb_pt[par][:, i8 * 128:(i8 + 1) * 128],
                        start=(kc == 0), stop=(kc == 31)),
                        waits=([ev_texp] + _w([pvfree[0] if kc == 0 else None]))
                        if i8 == 0 else ())
                kb.op("tensor", lambda e: e.drain(), inc=s_pe)
                ev_tpv = s_pe.now
                ev_pt_rd[par] = ev_tpv
            kb.op("vector", lambda e: e.tensor_copy(
                sb_un[0:65, 0:128], ps_pv[0:65, 0:128]), waits=[ev_tpv], inc=s_dv)
            ev_tev = s_dv.now
            kb.op("vector", lambda e: e.reciprocal(
                sb_nrr[0:1, 0:128], ps_pv[64:65, 0:128]), inc=s_dv)
            ev_trec = s_dv.now
            kb.op("tensor", lambda e: e.matmul(
                ps_pv[0:64, 0:128], sb_o64[:], sb_nrr[0:1, 0:128],
                start=True, stop=True), waits=[ev_tev, ev_trec], inc=s_pe)
            ev_tbc = s_pe.now
            kb.op("vector", lambda e, hp=hp, blk=blk: e.tensor_tensor(
                sb_cat[hp:hp + 64, blk + QMAIN: blk + QW],
                sb_un[0:64, 0:128], ps_pv[0:64, 0:128], ALU.mult),
                waits=[ev_tbc], inc=s_dv)
            pvfree[0] = s_dv.now
            if PH == 25 and h == 0:
                kb.wait("vector", pvfree[0])
                kb.wait("sync", (s_st, s_st.n))
                with nc.Block() as block:
                    kb.emit(block)
                return nc
    ps2.close()
    p2.close()

    if PH <= 2:
        kb.op("sync", lambda e: e.dma_start(out=d_dbg[:, 0:16384],
                                            in_=sb_cat[:, 0:16384]),
              waits=[(s_dv, s_dv.n)], inc=s_st, inck=16)
        kb.wait("sync", (s_st, s_st.n))
        with nc.Block() as block:
            kb.emit(block)
        return nc

    kb.barrier_all()
    p3 = contextlib.ExitStack()
    A3 = p3.enter_context
    sb_wo = A3(nc.sbuf_tensor("wo_sb", [128, 8 * D], BF16))
    sb_aof = A3(nc.sbuf_tensor("aof", [128, D], BF16))
    sb_aot = A3(nc.sbuf_tensor("aot", [128, D], BF16))
    sb_r1 = [A3(nc.sbuf_tensor(f"r1b{i}", [128, D], F32)) for i in range(2)]
    sb_xr = A3(nc.sbuf_tensor("xr", [128, D], F32))
    sb_xi8 = A3(nc.sbuf_tensor("xi8", [128, D], I8))
    sb_xqt = A3(nc.sbuf_tensor("xqt", [128, D], BF16))

    ps3 = contextlib.ExitStack()
    ps_o = ps3.enter_context(nc.psum_tensor("ps_o", [128, 1024], F32))
    ps_tb3 = ps3.enter_context(nc.psum_tensor("ps_tb3", [128, 1024], BF16))
    ps_sq3 = ps3.enter_context(nc.psum_tensor("ps_sq3", [128, 1024], F32))

    sb_xq = sb_q

    kb.op("gpsimd", lambda e: e.dma_start(
        out=cast3(sb_wo[:], d_wo, D)[0], in_=cast3(sb_wo[:], d_wo, D)[1]),
        inc=s_gq, inck=16)
    ev_wo = s_gq.now
    kb.wait("tensor", ev_wo)
    ev_r1_st_b = [None, None]
    ev_xt_use = [None, None]
    prev_aof = [None]
    for tt in range(NQT):
        bb = tt % 2
        for oc in range(8):
            for kc in range(8):
                kb.op("tensor", lambda e, tt=tt, oc=oc, kc=kc: e.matmul(
                    ps_o[:, oc * 128:(oc + 1) * 128],
                    sb_wo[:, kc * D + oc * 128: kc * D + (oc + 1) * 128],
                    sb_cat[:, kc * QW + tt * 128: kc * QW + (tt + 1) * 128],
                    start=(kc == 0), stop=(kc == 7)),
                    waits=_w([prev_aof[0]]) if (oc == 0 and kc == 0) else ())
        kb.op("tensor", lambda e: e.drain(), inc=s_pe)
        kb.op("vector", lambda e: e.tensor_copy(sb_aof[:], ps_o[:]),
              waits=[s_pe.now], inc=s_dv)
        ev_aof = s_dv.now
        prev_aof[0] = ev_aof
        for c in range(8):
            kb.op("tensor", lambda e, c=c: e.transpose(
                ps_tb3[:, c * 128:(c + 1) * 128], sb_aof[:, c * 128:(c + 1) * 128],
                sb_eye[:]), waits=[ev_aof] if c == 0 else ())
        kb.op("tensor", lambda e: e.drain(), inc=s_pe)
        kb.op("vector", lambda e: e.tensor_copy(sb_aot[:], ps_tb3[:]),
              waits=[s_pe.now], inc=s_dv)
        ev_aot = s_dv.now
        ev_ln2 = ln_cols(sb_aot[:], D, ps_sq3, 0, wv=[ev_aot], wa=[ev_aot])
        kb.op("sync", lambda e, tt=tt, bb=bb: e.dma_start(
            out=sb_xt[bb][:], in_=d_x[tt * 128:(tt + 1) * 128, :]),
            waits=_w([ev_xt_use[bb]]), inc=s_sy, inck=16)
        ev_x = s_sy.now
        kb.op("vector", lambda e, bb=bb: e.tensor_scalar(
            sb_r1[bb][:], sb_aot[:], sb_stats[:, 1:2], sb_stats[:, 2:3],
            ALU.mult, ALU.add),
            waits=[ev_ln2] + _w([ev_r1_st_b[bb]]), inc=s_dv)
        kb.op("vector", lambda e, bb=bb: e.tensor_add(
            sb_r1[bb][:], sb_r1[bb][:], sb_xt[bb][:]),
            waits=[ev_x, (s_dv, s_dv.n)], inc=s_dv)
        ev_r1 = s_dv.now
        ev_xt_use[bb] = ev_r1
        kb.op("sync", lambda e, tt=tt, bb=bb: e.dma_start(
            out=d_r1[tt * 128:(tt + 1) * 128, :], in_=sb_r1[bb][:]),
            waits=[ev_r1], inc=s_st, inck=16)
        ev_r1_st_b[bb] = (s_st, s_st.n)
        ev_ln3 = ln_cols(sb_r1[bb][:], D, ps_sq3, 8, wv=[ev_r1], wa=[ev_r1])
        kb.op("vector", lambda e, bb=bb: e.tensor_scalar(
            sb_xr[:], sb_r1[bb][:], sb_stats[:, 8:9], sb_stats[:, 9:10],
            ALU.subtract, ALU.mult), waits=[ev_ln3], inc=s_dv)
        kb.op("vector", lambda e: e.reduce_max(
            sb_stats[:, 16:17], sb_xr[:], axis=AX.X, apply_absolute_value=True),
            waits=[(s_dv, s_dv.n)], inc=s_dv)
        kb.op("vector", lambda e: e.reciprocal(sb_stats[:, 17:18], sb_stats[:, 16:17]),
            waits=[(s_dv, s_dv.n)], inc=s_dv)
        kb.op("vector", lambda e: e.tensor_scalar_mul(
            sb_stats[:, 17:18], sb_stats[:, 17:18], 127.0),
            waits=[(s_dv, s_dv.n)], inc=s_dv)
        kb.op("vector", lambda e: e.tensor_scalar_mul(
            sb_stats[:, 18:19], sb_stats[:, 16:17], R127), inc=s_dv)
        kb.op("vector", lambda e: e.tensor_scalar(
            sb_xi8[:], sb_xr[:], sb_stats[:, 17:18], None, ALU.mult),
            waits=[(s_dv, s_dv.n)], inc=s_dv)
        kb.op("vector", lambda e: e.tensor_scalar(
            sb_xqt[:], sb_xi8[:], sb_stats[:, 18:19], None, ALU.mult),
            waits=[(s_dv, s_dv.n)], inc=s_dv)
        ev_xqt = s_dv.now
        kb.wait("tensor", ev_xqt)
        for c in range(8):
            kb.op("tensor", lambda e, c=c: e.transpose(
                ps_tb3[:, c * 128:(c + 1) * 128], sb_xqt[:, c * 128:(c + 1) * 128],
                sb_eye[:]))
        kb.op("tensor", lambda e: e.drain(), inc=s_pe)
        kb.op("vector", lambda e, tt=tt: e.tensor_copy(
            sb_xq[:].rearrange("p (o q) -> p o q", q=QW)[:, :, tt * 128:(tt + 1) * 128],
            ps_tb3[:].rearrange("p (o q) -> p o q", q=128)),
            waits=[s_pe.now], inc=s_dv)
    ev_r1_all = (s_st, s_st.n)
    ps3.close()
    p3.close()
    cat_es.close()
    p2a.close()
    attn_es.close()

    if PH <= 3:
        kb.op("sync", lambda e: e.dma_start(out=d_dbg[:, 0:16384],
                                            in_=sb_xq[:, 0:16384]),
              waits=[(s_dv, s_dv.n)], inc=s_st, inck=16)
        kb.wait("sync", (s_st, s_st.n))
        with nc.Block() as block:
            kb.emit(block)
        return nc

    kb.barrier_all()
    p34 = contextlib.ExitStack()
    A4 = p34.enter_context
    sb_w1q = A4(nc.sbuf_tensor("w1q", [128, 8 * DH], BF16))
    sb_w2q = A4(nc.sbuf_tensor("w2q", [128, 32 * D], BF16))

    pq = contextlib.ExitStack()
    sb_stg = [pq.enter_context(nc.sbuf_tensor(f"stg{i}", [128, 2048], F32))
              for i in range(2)]
    sb_ti8 = pq.enter_context(nc.sbuf_tensor("ti8", [128, 2048], I8))

    ev_qz = [None, None]
    jobs = []
    for c in range(8):
        for half in range(2):
            jobs.append((d_w1t[c * 128:(c + 1) * 128, half * 2048:(half + 1) * 2048],
                         sb_w1q[:, c * DH + half * 2048: c * DH + (half + 1) * 2048], 0))
    for c in range(16):
        jobs.append((d_w2t[c * 256:(c + 1) * 256, :].rearrange("(t p) j -> p t j", p=128),
                     sb_w2q[:, c * 2048:(c + 1) * 2048], 1))
    for i, (src, dst, lane) in enumerate(jobs):
        b = i % 2
        sdst = sb_stg[b][:]
        if len(src.shape) == 3:
            sdst = sdst.rearrange("p (t j) -> p t j", j=1024)
        kb.op("sync", lambda e, src=src, b=b, sdst=sdst: e.dma_start(
            out=sdst, in_=src), waits=_w([ev_qz[b]]), inc=s_sy, inck=16)
        ev_ld = s_sy.now
        kb.op("vector", lambda e, b=b, lane=lane: e.tensor_scalar(
            sb_ti8[:], sb_stg[b][:], sb_wstb[:, lane:lane + 1], None, ALU.mult),
            waits=[ev_ld], inc=s_dv)
        kb.op("vector", lambda e, dst=dst: e.tensor_scalar(
            dst, sb_ti8[:], 1.0, -1.0, ALU.min, ALU.max), waits=[(s_dv, s_dv.n)],
            inc=s_dv)
        ev_qz[b] = s_dv.now
    pq.close()

    if PH <= 4:
        kb.op("sync", lambda e: e.dma_start(out=d_dbg[:, 0:12288],
                                            in_=sb_w1q[:, 0:12288]),
              waits=[(s_dv, s_dv.n)], inc=s_st, inck=16)
        kb.op("sync", lambda e: e.dma_start(out=d_dbg[:, 12288:16288],
                                            in_=sb_w2q[:, 0:4000]),
              inc=s_st, inck=16)
        kb.op("gpsimd", lambda e: e.dma_start(out=d_dbg[:, 16380:16384],
                                              in_=sb_wstb[:, 0:4]),
              waits=[(s_dv, s_dv.n)], inc=s_st, inck=16)
        kb.wait("sync", (s_st, s_st.n))
        with nc.Block() as block:
            kb.emit(block)
        return nc

    kb.barrier_all()
    p4 = contextlib.ExitStack()
    A5 = p4.enter_context
    sb_h = A5(nc.sbuf_tensor("h_sb", [128, 32 * HW], BF16))
    sb_c = A5(nc.sbuf_tensor("c_sb", [128, 32 * TF], BF16))
    sb_tmp = [A5(nc.sbuf_tensor(f"tmp{i}", [128, TF], F32)) for i in range(2)]
    sb_m = A5(nc.sbuf_tensor("m_sb", [128, TF], F32))
    sb_mn = A5(nc.sbuf_tensor("mn_sb", [128, TF], F32))
    sb_am = A5(nc.sbuf_tensor("am", [128, 4], F32))
    sb_amt = A5(nc.sbuf_tensor("amt", [128, 64], F32))
    sb_rw = A5(nc.sbuf_tensor("rw", [1, 2 * TF], F32))
    sb_scb = A5(nc.sbuf_tensor("scb", [128, TF], F32))
    sb_dsb = A5(nc.sbuf_tensor("dsb", [128, TF], F32))
    sb_i84 = A5(nc.sbuf_tensor("i84", [128, TF], I8))
    sb_fo = A5(nc.sbuf_tensor("fo", [128, 8 * TF], BF16))
    sb_fot = A5(nc.sbuf_tensor("fot", [128, D], BF16))
    sb_ob = A5(nc.sbuf_tensor("ob", [128, D], F32))

    ps4 = contextlib.ExitStack()
    ps_h = [ps4.enter_context(nc.psum_tensor(f"ps_h{i}", [128, HW], F32)) for i in range(2)]
    ps_2 = [ps4.enter_context(nc.psum_tensor(f"ps_2{i}", [128, TF], F32)) for i in range(2)]
    ps_tb4 = ps4.enter_context(nc.psum_tensor("ps_tb4", [128, 1024], BF16))
    ps_sq4 = ps4.enter_context(nc.psum_tensor("ps_sq4", [128, 1024], F32))
    ps_bc = ps4.enter_context(nc.psum_tensor("ps_bc", [128, TF], F32))

    kb.wait("sync", ev_r1_all)
    kb.op("vector", lambda e: e.memset(sb_amt[:], 0.0), inc=s_dv)
    ev_hq_prev = [None]
    ev_h_rd = [None, None]
    ev_2_rd = [None, None]
    ev_tb4_rd = None
    ev_ob_st = None
    ev_xt4_use = [None, None]
    prev_ln4 = [None]

    for (f0, w) in FT:
        pad_l = 1 if f0 == 0 else 0
        pad_r = 1 if f0 + w >= QW else 0
        hwid = w + 2 - pad_l - pad_r
        src0 = max(f0 - 1, 0)
        if pad_l or pad_r:
            view = sb_h[:].rearrange("p (o q) -> p o q", q=HW)
            if pad_l:
                kb.op("vector", lambda e, v=view: e.memset(v[:, :, 0:1], 0.0), inc=s_dv)
            if pad_r:
                kb.op("vector", lambda e, v=view, w=w: e.memset(
                    v[:, :, w + 1:w + 2], 0.0), inc=s_dv)
            ev_pad = s_dv.now
        else:
            ev_pad = None
        ev_hev = []
        for oc in range(32):
            ph = oc % 2
            for kc in range(8):
                kb.op("tensor", lambda e, oc=oc, kc=kc, ph=ph, src0=src0, hwid=hwid: e.matmul(
                    ps_h[ph][:, 0:hwid],
                    sb_w1q[:, kc * DH + oc * 128: kc * DH + (oc + 1) * 128],
                    sb_xq[:, kc * QW + src0: kc * QW + src0 + hwid],
                    start=(kc == 0), stop=(kc == 7)),
                    waits=_w([ev_h_rd[ph]]) if kc == 0 else ())
            kb.op("tensor", lambda e: e.drain(), inc=s_pe)
            kb.op("scalar", lambda e, oc=oc, ph=ph, pad_l=pad_l, hwid=hwid: e.activation(
                sb_h[:, oc * HW + pad_l: oc * HW + pad_l + hwid], ps_h[ph][:, 0:hwid],
                AF.Copy, scale=sb_wstb[:, 2:3]),
                waits=[s_pe.now] + _w([ev_pad if oc < 2 else None,
                                       ev_hq_prev[0] if oc == 0 else None]), inc=s_ac)
            ev_h_rd[ph] = s_ac.now
            ev_hev.append(s_ac.now)

        if PH == 41 and f0 == 0:
            kb.op("sync", lambda e: e.dma_start(out=d_dbg[:, 0:5440],
                                                in_=sb_h[:, 0:5440]),
                  waits=[(s_ac, s_ac.n), (s_dv, s_dv.n)], inc=s_st, inck=16)
            kb.wait("sync", (s_st, s_st.n))
            with nc.Block() as block:
                kb.emit(block)
            return nc
        ev_cs = []
        for oc in range(32):
            hbase = oc * HW
            kb.op("vector", lambda e, hbase=hbase, oc=oc, w=w: e.tensor_scalar(
                sb_tmp[0][:, 0:w], sb_h[:, hbase:hbase + w],
                sb_cw[:, oc * 3:oc * 3 + 1], sb_cb[:, oc:oc + 1],
                ALU.mult, ALU.add),
                waits=[ev_hev[oc]] + ([ev_cs[-1]] if ev_cs else []), inc=s_dv)
            kb.op("vector", lambda e, hbase=hbase, oc=oc, w=w: e.scalar_tensor_tensor(
                sb_tmp[1][:, 0:w], sb_h[:, hbase + 1:hbase + 1 + w],
                sb_cw[:, oc * 3 + 1:oc * 3 + 2], sb_tmp[0][:, 0:w],
                ALU.mult, ALU.add), waits=[(s_dv, s_dv.n)], inc=s_dv)
            kb.op("vector", lambda e, hbase=hbase, oc=oc, w=w: e.scalar_tensor_tensor(
                sb_tmp[0][:, 0:w], sb_h[:, hbase + 2:hbase + 2 + w],
                sb_cw[:, oc * 3 + 2:oc * 3 + 3], sb_tmp[1][:, 0:w],
                ALU.mult, ALU.add), waits=[(s_dv, s_dv.n)], inc=s_dv)
            kb.op("scalar", lambda e, oc=oc, w=w: e.activation(
                sb_c[:, oc * TF: oc * TF + w], sb_tmp[0][:, 0:w], AF.Silu),
                waits=[s_dv.now], inc=s_ac)
            ev_cs.append(s_ac.now)

        if PH == 42 and f0 == 0:
            kb.op("sync", lambda e: e.dma_start(out=d_dbg[:, 0:5376],
                                                in_=sb_c[:, 0:5376]),
                  waits=[(s_ac, s_ac.n), (s_dv, s_dv.n)], inc=s_st, inck=16)
            kb.wait("sync", (s_st, s_st.n))
            with nc.Block() as block:
                kb.emit(block)
            return nc
        kb.op("vector", lambda e, w=w: e.tensor_copy(
            sb_m[:, 0:w], sb_c[:, 0:w]), waits=[ev_cs[0]], inc=s_dv)
        kb.op("vector", lambda e, w=w: e.tensor_copy(
            sb_mn[:, 0:w], sb_c[:, 0:w]), inc=s_dv)
        for oc in range(1, 32):
            kb.op("vector", lambda e, oc=oc, w=w: e.tensor_max(
                sb_m[:, 0:w], sb_m[:, 0:w], sb_c[:, oc * TF: oc * TF + w]),
                waits=[ev_cs[oc], (s_dv, s_dv.n)], inc=s_dv)
            kb.op("vector", lambda e, oc=oc, w=w: e.tensor_tensor(
                sb_mn[:, 0:w], sb_mn[:, 0:w], sb_c[:, oc * TF: oc * TF + w],
                ALU.min), waits=[(s_dv, s_dv.n)], inc=s_dv)
        kb.op("vector", lambda e, w=w: e.tensor_scalar_mul(
            sb_mn[:, 0:w], sb_mn[:, 0:w], -1.0), waits=[(s_dv, s_dv.n)], inc=s_dv)
        kb.op("vector", lambda e, w=w: e.tensor_max(
            sb_m[:, 0:w], sb_m[:, 0:w], sb_mn[:, 0:w]), waits=[(s_dv, s_dv.n)],
            inc=s_dv)
        ev_m = s_dv.now
        nj = (w + 127) // 128
        ev_rwj = [None]
        for j in range(nj):
            wj = min(128, w - j * 128)
            kb.op("tensor", lambda e, j=j, wj=wj: e.transpose(
                ps_sq4[0:wj, 0:128], sb_m[:, j * 128: j * 128 + wj], sb_eyef[:]),
                waits=([ev_m] + _w([prev_ln4[0], ev_tb4_rd]) if j == 0
                       else [ev_m] + _w([ev_rwj[0]])),
                inc=s_pe)
            kb.op("vector", lambda e, wj=wj: e.reduce_max(
                sb_am[0:wj, 0:1], ps_sq4[0:wj, 0:128], axis=AX.X), waits=[s_pe.now],
                inc=s_dv)
            kb.op("vector", lambda e, wj=wj: e.reciprocal(
                sb_am[0:wj, 1:2], sb_am[0:wj, 0:1]), waits=[(s_dv, s_dv.n)], inc=s_dv)
            kb.op("vector", lambda e, wj=wj: e.tensor_scalar_mul(
                sb_am[0:wj, 1:2], sb_am[0:wj, 1:2], 127.0), waits=[(s_dv, s_dv.n)],
                inc=s_dv)
            kb.op("vector", lambda e, wj=wj: e.tensor_scalar_mul(
                sb_am[0:wj, 2:3], sb_am[0:wj, 0:1], R127), inc=s_dv)
            kb.op("vector", lambda e, wj=wj: e.tensor_copy(
                sb_amt[0:wj, 0:1], sb_am[0:wj, 1:2]), waits=[(s_dv, s_dv.n)], inc=s_dv)
            kb.op("vector", lambda e, wj=wj: e.tensor_copy(
                sb_amt[0:wj, 32:33], sb_am[0:wj, 2:3]), waits=[(s_dv, s_dv.n)],
                inc=s_dv)
            ev_amc = s_dv.now
            kb.op("tensor", lambda e, wj=wj: e.transpose(
                ps_sq4[0:64, 128:128 + wj], sb_amt[0:wj, 0:64],
                sb_eyef[0:wj, 0:wj]), waits=[ev_amc], inc=s_pe)
            ev_amtt = s_pe.now
            kb.op("vector", lambda e, j=j, wj=wj: e.tensor_copy(
                sb_rw[0:1, j * 128: j * 128 + wj],
                ps_sq4[0:1, 128:128 + wj]), waits=[ev_amtt])
            kb.op("vector", lambda e, j=j, wj=wj: e.tensor_copy(
                sb_rw[0:1, TF + j * 128: TF + j * 128 + wj],
                ps_sq4[32:33, 128:128 + wj]), inc=s_dv)
            ev_rwj[0] = s_dv.now
        ev_rw = s_dv.now
        kb.op("tensor", lambda e, w=w: e.matmul(
            ps_bc[:, 0:w], sb_o1f[:], sb_rw[0:1, 0:w], start=True, stop=True),
            waits=[ev_rw])
        kb.op("tensor", lambda e: e.drain(), inc=s_pe)
        kb.op("scalar", lambda e, w=w: e.activation(
            sb_scb[:, 0:w], ps_bc[:, 0:w], AF.Copy), waits=[s_pe.now], inc=s_ac)
        kb.op("tensor", lambda e, w=w: e.matmul(
            ps_bc[:, 0:w], sb_o1f[:], sb_rw[0:1, TF:TF + w], start=True, stop=True),
            waits=[s_ac.now])
        kb.op("tensor", lambda e: e.drain(), inc=s_pe)
        kb.op("scalar", lambda e, w=w: e.activation(
            sb_dsb[:, 0:w], ps_bc[:, 0:w], AF.Copy), waits=[s_pe.now], inc=s_ac)
        ev_bcx = s_ac.now

        if PH == 43 and f0 == 0:
            kb.op("gpsimd", lambda e: e.dma_start(out=d_dbg[:, 0:168],
                                                  in_=sb_scb[:, 0:168]),
                  waits=[(s_ac, s_ac.n), (s_dv, s_dv.n)], inc=s_st, inck=16)
            kb.op("gpsimd", lambda e: e.dma_start(out=d_dbg[:, 168:336],
                                                  in_=sb_dsb[:, 0:168]),
                  inc=s_st, inck=16)
            kb.op("gpsimd", lambda e: e.dma_start(out=d_dbg[0:1, 336:672],
                                                  in_=sb_rw[0:1, 0:336]),
                  inc=s_st, inck=16)
            kb.wait("sync", (s_st, s_st.n))
            with nc.Block() as block:
                kb.emit(block)
            return nc
        ev_hq = []
        for oc in range(32):
            kb.op("vector", lambda e, oc=oc, w=w: e.tensor_tensor(
                sb_i84[:, 0:w], sb_c[:, oc * TF: oc * TF + w], sb_scb[:, 0:w],
                ALU.mult), waits=([ev_bcx] if oc == 0 else []) + [(s_dv, s_dv.n)],
                inc=s_dv)
            kb.op("vector", lambda e, oc=oc, w=w: e.tensor_tensor(
                sb_h[:, oc * HW: oc * HW + w], sb_i84[:, 0:w], sb_dsb[:, 0:w],
                ALU.mult), waits=[(s_dv, s_dv.n)], inc=s_dv)
            ev_hq.append(s_dv.now)

        if PH == 44 and f0 == 0:
            kb.op("sync", lambda e: e.dma_start(out=d_dbg[:, 0:5440],
                                                in_=sb_h[:, 0:5440]),
                  waits=[(s_ac, s_ac.n), (s_dv, s_dv.n)], inc=s_st, inck=16)
            kb.wait("sync", (s_st, s_st.n))
            with nc.Block() as block:
                kb.emit(block)
            return nc
        ev_hq_prev[0] = ev_hq[-1]
        ev_fo = []
        for oc2 in range(8):
            p2b = oc2 % 2
            for kc in range(32):
                kb.op("tensor", lambda e, oc2=oc2, kc=kc, p2b=p2b, w=w: e.matmul(
                    ps_2[p2b][:, 0:w],
                    sb_w2q[:, kc * D + oc2 * 128: kc * D + (oc2 + 1) * 128],
                    sb_h[:, kc * HW: kc * HW + w],
                    start=(kc == 0), stop=(kc == 31)),
                    waits=(_w([ev_2_rd[p2b]]) + [ev_hq[-1]]) if kc == 0 else ())
            kb.op("tensor", lambda e: e.drain(), inc=s_pe)
            kb.op("scalar", lambda e, oc2=oc2, p2b=p2b, w=w: e.activation(
                sb_fo[:, oc2 * TF: oc2 * TF + w], ps_2[p2b][:, 0:w], AF.Copy,
                scale=sb_wstb[:, 3:4]), waits=[s_pe.now], inc=s_ac)
            ev_2_rd[p2b] = s_ac.now
            ev_fo.append(s_ac.now)

        if PH == 45 and f0 == 0:
            kb.op("sync", lambda e: e.dma_start(out=d_dbg[:, 0:1344],
                                                in_=sb_fo[:, 0:1344]),
                  waits=[(s_ac, s_ac.n), (s_dv, s_dv.n)], inc=s_st, inck=16)
            kb.wait("sync", (s_st, s_st.n))
            with nc.Block() as block:
                kb.emit(block)
            return nc
        for j in range(nj):
            wj = min(128, w - j * 128)
            g0 = f0 + j * 128
            bb = j % 2
            for oc2 in range(8):
                kb.op("tensor", lambda e, oc2=oc2, j=j, wj=wj: e.transpose(
                    ps_tb4[0:wj, oc2 * 128:(oc2 + 1) * 128],
                    sb_fo[:, oc2 * TF + j * 128: oc2 * TF + j * 128 + wj],
                    sb_eye[:]),
                    waits=([ev_fo[-1]] + _w([ev_tb4_rd])) if oc2 == 0 else ())
            kb.op("tensor", lambda e: e.drain(), inc=s_pe)
            kb.op("vector", lambda e, wj=wj: e.tensor_copy(
                sb_fot[0:wj, :], ps_tb4[0:wj, :]), waits=[s_pe.now], inc=s_dv)
            ev_fot = s_dv.now
            ev_tb4_rd = ev_fot
            ev_ln4 = ln_cols(sb_fot[0:wj, :], D, ps_sq4, 0, P=wj,
                             wv=[ev_fot], wa=[ev_fot])
            prev_ln4[0] = ev_ln4
            kb.op("sync", lambda e, g0=g0, wj=wj, bb=bb: e.dma_start(
                out=sb_xt[bb][0:wj, :], in_=d_r1[g0:g0 + wj, :]),
                waits=_w([ev_xt4_use[bb]]), inc=s_sy, inck=16)
            ev_rr = s_sy.now
            kb.op("vector", lambda e, wj=wj: e.tensor_scalar(
                sb_ob[0:wj, :], sb_fot[0:wj, :], sb_stats[0:wj, 1:2],
                sb_stats[0:wj, 2:3], ALU.mult, ALU.add),
                waits=[ev_ln4] + _w([ev_ob_st]), inc=s_dv)
            kb.op("vector", lambda e, wj=wj, bb=bb: e.tensor_add(
                sb_ob[0:wj, :], sb_ob[0:wj, :], sb_xt[bb][0:wj, :]),
                waits=[ev_rr, (s_dv, s_dv.n)], inc=s_dv)
            ev_ob = s_dv.now
            ev_xt4_use[bb] = ev_ob
            kb.op("sync", lambda e, g0=g0, wj=wj: e.dma_start(
                out=d_y[g0:g0 + wj, :], in_=sb_ob[0:wj, :]),
                waits=[ev_ob], inc=s_st, inck=16)
            ev_ob_st = (s_st, s_st.n)

    kb.wait("sync", (s_st, s_st.n))

    with nc.Block() as block:
        kb.emit(block)
    ps4.close()
    p4.close()
    p34.close()
    st.close()
    return nc


_CACHE = {}


def kernel(L0_attn_x, L0_attn_Wq, L0_attn_Wk, L0_attn_Wv, L0_attn_Wo,
           L0_attn_W_down_kv, L0_attn_W_up_k, L0_attn_W_up_v,
           L0_ffn_w1, L0_ffn_conv_w, L0_ffn_conv_b, L0_ffn_w2):
    x = np.ascontiguousarray(np.asarray(L0_attn_x, dtype=np.float32))
    wq = np.ascontiguousarray(np.asarray(L0_attn_Wq, dtype=np.float32))
    wk = np.ascontiguousarray(np.asarray(L0_attn_Wk, dtype=np.float32))
    wv = np.ascontiguousarray(np.asarray(L0_attn_Wv, dtype=np.float32))
    wo = np.ascontiguousarray(np.asarray(L0_attn_Wo, dtype=np.float32))
    wdn = np.ascontiguousarray(np.asarray(L0_attn_W_down_kv, dtype=np.float32))
    wuk = np.ascontiguousarray(np.asarray(L0_attn_W_up_k, dtype=np.float32))
    wuv = np.ascontiguousarray(np.asarray(L0_attn_W_up_v, dtype=np.float32))
    w1t = np.ascontiguousarray(np.asarray(L0_ffn_w1, dtype=np.float32).T)
    w2t = np.ascontiguousarray(np.asarray(L0_ffn_w2, dtype=np.float32).T)
    cw = np.ascontiguousarray(
        np.asarray(L0_ffn_conv_w, dtype=np.float32).reshape(32, 128, 3)
        .transpose(1, 0, 2).reshape(128, 96))
    cb = np.ascontiguousarray(
        np.asarray(L0_ffn_conv_b, dtype=np.float32).reshape(32, 128).T)
    eye = np.eye(128, dtype=np.float32).astype(ml_dtypes.bfloat16)
    eyef = np.eye(128, dtype=np.float32)
    dk = np.arange(128, dtype=np.float32)[:, None]
    mm = np.arange(128, dtype=np.float32)[None, :]
    dint = np.ascontiguousarray(dk - mm)

    if "nc" not in _CACHE:
        _CACHE["nc"] = build_nc()
    nc = _CACHE["nc"]

    in_maps = []
    for c in range(8):
        b, hf = c // 2, c % 2
        rot = 0 if hf == 0 else 1920
        sgk = -1.0 if hf == 0 else 1.0
        kpos = ((np.arange(S) + rot) % S).astype(np.float32)
        rows3 = np.concatenate([kpos[:QW], -kpos[:QW], -sgk * kpos[:QW]])[None, :]
        rkf = (sgk * kpos[QW:])[None, :]
        in_maps.append({
            "xb": np.ascontiguousarray(np.roll(x[b], -rot, axis=0)),
            "wq": wq, "wk": wk, "wv": wv, "wo": wo, "wdn": wdn,
            "wuk": wuk, "wuv": wuv, "w1t": w1t, "w2t": w2t,
            "cw": cw, "cb": cb,
            "rows3": np.ascontiguousarray(rows3.astype(np.float32)),
            "rkf": np.ascontiguousarray(rkf.astype(np.float32)),
            "dint": dint, "eye": eye, "eyef": eyef,
        })
    res = run_bass_kernel_spmd(nc, in_maps, core_ids=list(range(8)))
    out = np.empty((4, S, D), dtype=np.float32)
    for c in range(8):
        b, hf = c // 2, c % 2
        y = np.asarray(res.results[c]["y"])
        if hf == 0:
            out[b, 0:2048] = y[0:2048]
        else:
            out[b, 2048:4096] = y[128:2176]
    return out
